# version 01 (final)
# baseline (speedup 1.0000x reference)
import numpy as np
import ml_dtypes
import contextlib
import concourse.bass as bass
import concourse.mybir as mybir
from concourse.bass_utils import run_bass_kernel_spmd

F32 = mybir.dt.float32
BF16 = mybir.dt.bfloat16
I32 = mybir.dt.int32
I8 = mybir.dt.int8
AF = mybir.ActivationFunctionType
ALU = mybir.AluOpType
BF = ml_dtypes.bfloat16

D = 2048
NC = 16
TOK = 2048
TT = 4
NOP = 336
NOPS = 336
EPS_LN = 1e-5
EPS_F32 = float(np.finfo(np.float32).eps)


class P:

    def __init__(self, nc, block, stack):
        self.nc = nc
        self.block = block
        self.stack = stack
        self.sems = {}
        self.cnt = {}

    def sem(self, name):
        if name not in self.sems:
            self.sems[name] = self.stack.enter_context(self.nc.semaphore(name))
            self.cnt[name] = 0
        return self.sems[name]

    def inc(self, instr, name, n=1):
        instr.then_inc(self.sem(name), n)
        self.cnt[name] += n
        return instr

    def wait(self, eng, name):
        eng.wait_ge(self.sem(name), self.cnt[name])


def mm(p, t, out, lhsT, rhs, start, stop, nop=NOP):
    t.matmul(out, lhsT, rhs, start=start, stop=stop)
    t.nop(cycle_cnt=nop)


def tr(p, t, out, in_, ident, nop=NOPS):
    t.transpose(out, in_, ident)
    t.nop(cycle_cnt=nop)



def emit_consts(p, ident, ones_col, ones_row, eps_row=None):
    @p.block.gpsimd
    def _(g):
        g.memset(ident[:], 1.0)
        g.affine_select(ident[:], ident[:], [[1, 128]], ALU.is_equal, 0.0,
                        base=0, channel_multiplier=-1)
        g.memset(ones_col[:], 1.0)
        g.memset(ones_row[:], 1.0)
        if eps_row is not None:
            g.memset(eps_row[0:1, 0:1], EPS_LN)
            g.memset(eps_row[0:1, 1:2], EPS_F32)
        p.inc(g.engine_nop(), "gp")


def emit_load_xs(p, xs, x_dram):
    @p.block.gpsimd
    def _(g):
        for dc in range(NC):
            p.inc(g.dma_start(out=xs[:, dc * TOK:(dc + 1) * TOK],
                              in_=x_dram[dc * 128:(dc + 1) * 128, :]), "qg", 16)


def emit_store_xs(p, xs, x_dram):
    @p.block.gpsimd
    def _(g):
        for w in ("pe", "act", "dve"):
            p.wait(g, w)
        for dc in range(NC):
            p.inc(g.dma_start(out=x_dram[dc * 128:(dc + 1) * 128, :],
                              in_=xs[:, dc * TOK:(dc + 1) * TOK]), "qg", 16)


def emit_ln(p, xs, sq, ab, tmp_row, ones_col, psum, eps_row, rmsfold=False):
    sums = psum[0:1, 3072:3584]
    sumsq = psum[0:1, 2560:3072]
    for tt in range(TT):
        t0 = tt * 512

        @p.block.vector
        def _(v, t0=t0):
            p.wait(v, "pe")
            p.wait(v, "act")
            for dc in range(NC):
                xt = xs[:, dc * TOK + t0:dc * TOK + t0 + 512]
                v.tensor_mul(sq[:, dc * 512:(dc + 1) * 512], xt, xt)
            p.inc(v.engine_nop(), "dve")

        @p.block.tensor
        def _(t, t0=t0):
            p.wait(t, "dve")
            for dc in range(NC):
                xt = xs[:, dc * TOK + t0:dc * TOK + t0 + 512]
                mm(p, t, sums, ones_col[:], xt, dc == 0, dc == NC - 1)
            for dc in range(NC):
                mm(p, t, sumsq, ones_col[:],
                   sq[:, dc * 512:(dc + 1) * 512], dc == 0, dc == NC - 1)
            p.inc(t.nop(), "pe")

        @p.block.vector
        def _(v, t0=t0):
            p.wait(v, "pe")
            m_sl = tmp_row[0:1, 0:512]
            v_sl = tmp_row[0:1, 512:1024]
            r_sl = tmp_row[0:1, 1024:1536]
            v.tensor_scalar_mul(m_sl, sums, 1.0 / D)
            v.tensor_scalar_mul(v_sl, sumsq, 1.0 / D)
            v.tensor_mul(r_sl, m_sl, m_sl)
            v.tensor_sub(v_sl, v_sl, r_sl)
            p.inc(v.engine_nop(), "dve")

        @p.block.scalar
        def _(s, t0=t0):
            p.wait(s, "dve")
            a_sl = ab[0:1, t0:t0 + 512]
            v_sl = tmp_row[0:1, 512:1024]
            r_sl = tmp_row[0:1, 1024:1536]
            s.activation(r_sl, v_sl, AF.Ln, bias=eps_row[0:1, 0:1])
            p.inc(s.activation(a_sl, r_sl, AF.Exp, scale=-0.5), "act")

        if rmsfold:
            @p.block.vector
            def _(v, t0=t0):
                p.wait(v, "act")
                v_sl = tmp_row[0:1, 512:1024]
                r_sl = tmp_row[0:1, 1024:1536]
                w_sl = tmp_row[0:1, 1536:2048]
                v.tensor_scalar_add(r_sl, v_sl, eps_row[0:1, 0:1])
                v.reciprocal(w_sl, r_sl)
                v.tensor_mul(w_sl, w_sl, v_sl)
                p.inc(v.engine_nop(), "dve")

            @p.block.scalar
            def _(s, t0=t0):
                p.wait(s, "dve")
                w_sl = tmp_row[0:1, 1536:2048]
                r_sl = tmp_row[0:1, 1024:1536]
                s.activation(r_sl, w_sl, AF.Ln, bias=eps_row[0:1, 1:2])
                p.inc(s.activation(w_sl, r_sl, AF.Exp, scale=-0.5), "act")

            @p.block.vector
            def _(v, t0=t0):
                p.wait(v, "act")
                a_sl = ab[0:1, t0:t0 + 512]
                w_sl = tmp_row[0:1, 1536:2048]
                v.tensor_mul(a_sl, a_sl, w_sl)
                p.inc(v.engine_nop(), "dve")

        @p.block.vector
        def _(v, t0=t0):
            p.wait(v, "act")
            a_sl = ab[0:1, t0:t0 + 512]
            b_sl = ab[0:1, TOK + t0:TOK + t0 + 512]
            m_sl = tmp_row[0:1, 0:512]
            v.tensor_mul(b_sl, m_sl, a_sl)
            p.inc(v.engine_nop(), "dve")


def emit_ln_apply(p, xs, ab, xn, ones_row, psum, tt):
    t0 = tt * 512
    bA = psum[:, 2560:3072]
    bB = psum[:, 3072:3584]

    @p.block.tensor
    def _(t):
        p.wait(t, "dve")
        p.wait(t, "act")
        mm(p, t, bA, ones_row[:], ab[0:1, t0:t0 + 512], True, True, nop=2700)
        mm(p, t, bB, ones_row[:], ab[0:1, TOK + t0:TOK + t0 + 512], True, True,
           nop=2700)
        p.inc(t.nop(), "pe")

    @p.block.vector
    def _(v):
        p.wait(v, "pe")
        for dc in range(NC):
            xt = xs[:, dc * TOK + t0:dc * TOK + t0 + 512]
            xo = xn[:, dc * 512:(dc + 1) * 512]
            v.tensor_mul(xo, xt, bA)
            v.tensor_sub(xo, xo, bB)
        p.inc(v.engine_nop(), "dve")


def emit_ffn(p, xs, ab, xn, h_sb, wbuf, psum, ones_row, kind, OC,
             wg_dram, wu_dram, wd_dram, sg=None, su=None, sd=None):
    for tt in range(TT):
        t0 = tt * 512
        emit_ln_apply(p, xs, ab, xn, ones_row, psum, tt)

        for oc in range(OC):
            par = oc % 2
            gbank = psum[:, par * 1024:par * 1024 + 512]
            ubank = psum[:, par * 1024 + 512:par * 1024 + 1024]
            wg_sl = wbuf[:, par * 4096:par * 4096 + 2048]
            wu_sl = wbuf[:, par * 4096 + 2048:par * 4096 + 4096]

            @p.block.gpsimd
            def _(g, oc=oc, wg_sl=wg_sl, wu_sl=wu_sl):
                p.wait(g, "pe")
                for (w_d, sl) in ((wg_dram, wg_sl), (wu_dram, wu_sl)):
                    wr = w_d.rearrange("(c k) o -> k c o", k=128)
                    p.inc(g.dma_start(
                        out=sl.rearrange("k (c m) -> k c m", c=16),
                        in_=wr[:, :, oc * 128:(oc + 1) * 128]), "qg", 16)

            @p.block.tensor
            def _(t, oc=oc, gbank=gbank, ubank=ubank, wg_sl=wg_sl, wu_sl=wu_sl):
                p.wait(t, "qg")
                p.wait(t, "dve")
                for dc in range(NC):
                    mm(p, t, gbank, wg_sl[:, dc * 128:(dc + 1) * 128],
                       xn[:, dc * 512:(dc + 1) * 512], dc == 0, dc == NC - 1)
                for dc in range(NC):
                    mm(p, t, ubank, wu_sl[:, dc * 128:(dc + 1) * 128],
                       xn[:, dc * 512:(dc + 1) * 512], dc == 0, dc == NC - 1)
                p.inc(t.nop(), "pe")

            @p.block.scalar
            def _(s, oc=oc, gbank=gbank):
                p.wait(s, "pe")
                if kind == "int8":
                    p.inc(s.activation(h_sb[:, oc * 512:(oc + 1) * 512], gbank,
                                       AF.Silu, scale=sg[:, oc:oc + 1]), "act")
                else:
                    p.inc(s.activation(h_sb[:, oc * 512:(oc + 1) * 512], gbank,
                                       AF.Silu), "act")

            @p.block.vector
            def _(v, oc=oc, ubank=ubank):
                p.wait(v, "act")
                hsl = h_sb[:, oc * 512:(oc + 1) * 512]
                if kind == "int8":
                    v.scalar_tensor_tensor(hsl, ubank, su[:, oc:oc + 1], hsl,
                                           ALU.mult, ALU.mult)
                else:
                    v.tensor_mul(hsl, ubank, hsl)
                p.inc(v.engine_nop(), "dve")

        for grp in range(4):
            for hc in range(OC):
                wd_sl = wbuf[:, (hc % 2) * 4096:(hc % 2) * 4096 + 512]

                @p.block.gpsimd
                def _(g, hc=hc, grp=grp, wd_sl=wd_sl):
                    p.wait(g, "pe")
                    wr = wd_dram.rearrange("(c k) o -> k c o", k=128)
                    p.inc(g.dma_start(
                        out=wd_sl.rearrange("k (c m) -> k c m", c=4),
                        in_=wr[:, hc:hc + 1, grp * 512:(grp + 1) * 512]
                        .rearrange("k one (c m) -> k (one c) m", c=4)), "qg", 16)

                @p.block.tensor
                def _(t, hc=hc, grp=grp, wd_sl=wd_sl):
                    p.wait(t, "qg")
                    if hc == 0:
                        p.wait(t, "dve")
                    for m in range(4):
                        mm(p, t, psum[:, m * 512:(m + 1) * 512],
                           wd_sl[:, m * 128:(m + 1) * 128],
                           h_sb[:, hc * 512:(hc + 1) * 512],
                           hc == 0, hc == OC - 1)
                    p.inc(t.nop(), "pe")

            @p.block.vector
            def _(v, grp=grp, t0=t0):
                p.wait(v, "pe")
                for m in range(4):
                    dc = grp * 4 + m
                    xsl = xs[:, dc * TOK + t0:dc * TOK + t0 + 512]
                    bank = psum[:, m * 512:(m + 1) * 512]
                    if kind == "int8":
                        v.scalar_tensor_tensor(xsl, bank, sd[:, dc:dc + 1],
                                               xsl, ALU.mult, ALU.add)
                    else:
                        v.tensor_add(xsl, xsl, bank)
                p.inc(v.engine_nop(), "dve")



def build_launch_W():
    nc = bass.Bass("TRN2", target_bir_lowering=False, debug=False)
    gidx = nc.dram_tensor("gidx", [256, 5504], I32, kind="ExternalInput").ap()
    uidx = nc.dram_tensor("uidx", [256, 5504], I32, kind="ExternalInput").ap()
    didx = nc.dram_tensor("didx", [688, 2048], I32, kind="ExternalInput").ap()
    lut = nc.dram_tensor("lut", [128, 16], F32, kind="ExternalInput").ap()
    wg = nc.dram_tensor("wg", [256, 5504], BF16, kind="ExternalOutput").ap()
    wu = nc.dram_tensor("wu", [256, 5504], BF16, kind="ExternalOutput").ap()
    wd = nc.dram_tensor("wd", [688, 2048], BF16, kind="ExternalOutput").ap()

    with contextlib.ExitStack() as stack:
        ctx = stack.enter_context
        idxt = ctx(nc.sbuf_tensor("idxt", [128, 5504], BF16))
        acc = ctx(nc.sbuf_tensor("acc", [128, 5504], F32))
        accb = ctx(nc.sbuf_tensor("accb", [128, 5504], BF16))
        lutt = ctx(nc.sbuf_tensor("lutt", [128, 16], F32))
        luth = ctx(nc.sbuf_tensor("luth", [128, 16], F32))
        block = ctx(nc.Block())
        p = P(nc, block, stack)
        for s_ in ("qg", "dve"):
            p.sem(s_)

        @p.block.gpsimd
        def _(g):
            p.inc(g.dma_start(out=lutt[:], in_=lut[:, :]), "qg", 16)

        @p.block.vector
        def _(v):
            p.wait(v, "qg")
            v.tensor_scalar_mul(luth[:], lutt[:], 0.5)
            p.inc(v.engine_nop(), "dve")

        jobs = []
        for r in range(2):
            jobs.append((gidx[r * 128:(r + 1) * 128, :],
                         wg[r * 128:(r + 1) * 128, :], 128, 5504, lutt))
            jobs.append((uidx[r * 128:(r + 1) * 128, :],
                         wu[r * 128:(r + 1) * 128, :], 128, 5504, lutt))
        for r in range(6):
            h = min(128, 688 - r * 128)
            jobs.append((didx[r * 128:r * 128 + h, :],
                         wd[r * 128:r * 128 + h, :], h, 2048, luth))

        for (src, dst, h, width, lt) in jobs:
            @p.block.gpsimd
            def _(g, src=src, h=h, width=width):
                p.wait(g, "dve")
                p.inc(g.dma_start(out=idxt[0:h, 0:width], in_=src), "qg", 16)

            @p.block.vector
            def _(v, h=h, width=width, lt=lt):
                p.wait(v, "qg")
                for j in range(16):
                    if j == 0:
                        v.tensor_scalar(acc[0:h, 0:width], idxt[0:h, 0:width],
                                        float(j), lt[0:h, j:j + 1],
                                        ALU.is_equal, ALU.mult)
                    else:
                        v.tensor_scalar(accb[0:h, 0:width], idxt[0:h, 0:width],
                                        float(j), lt[0:h, j:j + 1],
                                        ALU.is_equal, ALU.mult)
                        v.tensor_add(acc[0:h, 0:width], acc[0:h, 0:width],
                                     accb[0:h, 0:width])
                v.tensor_copy(accb[0:h, 0:width], acc[0:h, 0:width])
                p.inc(v.engine_nop(), "dve")

            @p.block.gpsimd
            def _(g, dst=dst, h=h, width=width):
                p.wait(g, "dve")
                p.inc(g.dma_start(out=dst, in_=accb[0:h, 0:width]), "qg", 16)

        @p.block.gpsimd
        def _(g):
            p.wait(g, "qg")

    return nc



def build_launch_A(upto=99):
    nc = bass.Bass("TRN2", target_bir_lowering=False, debug=False)
    xin = nc.dram_tensor("xin", [D, TOK], F32, kind="ExternalInput").ap()
    wgt = nc.dram_tensor("wgt", [D, 5504], BF16, kind="ExternalInput").ap()
    wut = nc.dram_tensor("wut", [D, 5504], BF16, kind="ExternalInput").ap()
    wdt = nc.dram_tensor("wdt", [5504, D], BF16, kind="ExternalInput").ap()
    wqt = nc.dram_tensor("wqt", [D, D], F32, kind="ExternalInput").ap()
    wkt = nc.dram_tensor("wkt", [D, D], F32, kind="ExternalInput").ap()
    wvt = nc.dram_tensor("wvt", [D, D], F32, kind="ExternalInput").ap()
    wgat = nc.dram_tensor("wgat", [D, D], F32, kind="ExternalInput").ap()

    x1 = nc.dram_tensor("x1", [D, TOK], BF16, kind="ExternalOutput").ap()
    qT = nc.dram_tensor("qT", [D, TOK], BF16, kind="ExternalOutput").ap()
    kT = nc.dram_tensor("kT", [D, TOK], BF16, kind="ExternalOutput").ap()
    gT = nc.dram_tensor("gT", [D, TOK], BF16, kind="ExternalOutput").ap()
    ktm = nc.dram_tensor("ktm", [TOK, D], BF16, kind="ExternalOutput").ap()
    vtm = nc.dram_tensor("vtm", [TOK, D], BF16, kind="ExternalOutput").ap()
    slo = nc.dram_tensor("slo", [D, D], BF16, kind="ExternalOutput").ap()
    shi = nc.dram_tensor("shi", [D, D], BF16, kind="ExternalOutput").ap()

    with contextlib.ExitStack() as stack:
        ctx = stack.enter_context
        xs = ctx(nc.sbuf_tensor("xs", [128, NC * TOK], BF16))
        scratch = ctx(nc.sbuf_tensor("scratch", [128, 32768], BF16))
        xn = ctx(nc.sbuf_tensor("xn", [128, NC * 512], BF16))
        sq = ctx(nc.sbuf_tensor("sq", [128, NC * 512], BF16))
        wbuf = ctx(nc.sbuf_tensor("wbuf", [128, 8192], BF16))
        ab = ctx(nc.sbuf_tensor("ab", [1, 2 * TOK], F32))
        tmp_row = ctx(nc.sbuf_tensor("tmp_row", [1, 2048], F32))
        eps_row = ctx(nc.sbuf_tensor("eps_row", [1, 2], F32))
        ident = ctx(nc.sbuf_tensor("ident", [128, 128], BF16))
        ones_col = ctx(nc.sbuf_tensor("ones_col", [128, 1], BF16))
        ones_row = ctx(nc.sbuf_tensor("ones_row", [1, 128], F32))
        ovec = ctx(nc.sbuf_tensor("ovec", [128, 512], BF16))
        otr = ctx(nc.sbuf_tensor("otr", [128, 512], BF16))
        psum = ctx(nc.psum_tensor("psum", [128, 3584], F32))
        psb = ctx(nc.psum_tensor("psb", [128, 512], BF16))
        block = ctx(nc.Block())
        p = P(nc, block, stack)
        for s_ in ("qg", "pe", "act", "dve", "gp"):
            p.sem(s_)

        h_sb = scratch[:, 0:43 * 512]
        kblk = scratch[:, 0:16384]
        vblk = scratch[:, 16384:32768]

        emit_consts(p, ident, ones_col, ones_row, eps_row)
        emit_load_xs(p, xs, xin)

        @p.block.vector
        def _(v):
            p.wait(v, "qg")
            p.wait(v, "gp")
            p.inc(v.engine_nop(), "dve")

        if upto >= 1:
            emit_ln(p, xs, sq, ab, tmp_row, ones_col, psum, eps_row, rmsfold=False)
        if upto >= 2:
            emit_ffn(p, xs, ab, xn, h_sb, wbuf, psum, ones_row, "ghost", 43,
                     wgt, wut, wdt)
        emit_store_xs(p, xs, x1)

        if upto >= 3:
            emit_ln(p, xs, sq, ab, tmp_row, ones_col, psum, eps_row, rmsfold=True)

        projs = [(wqt, qT, AF.Silu, None),
                 (wkt, kT, AF.Silu, ktm),
                 (wvt, None, AF.Silu, vtm),
                 (wgat, gT, AF.Sigmoid, None)]

        for tt in range(TT if upto >= 4 else 0):
            emit_ln_apply(p, xs, ab, xn, ones_row, psum, tt)
            for (w_d, out_d, act, tm_d) in projs:
                for oc in range(NC):
                    par = oc % 2
                    wsl = wbuf[:, par * 2048:par * 2048 + 2048]
                    bank = psum[:, par * 512:par * 512 + 512]

                    @p.block.gpsimd
                    def _(g, w_d=w_d, oc=oc, wsl=wsl):
                        p.wait(g, "pe")
                        wr = w_d.rearrange("(c k) o -> k c o", k=128)
                        p.inc(g.dma_start(
                            out=wsl.rearrange("k (c m) -> k c m", c=16),
                            in_=wr[:, :, oc * 128:(oc + 1) * 128]), "qg", 16)

                    @p.block.tensor
                    def _(t, oc=oc, wsl=wsl, bank=bank):
                        p.wait(t, "qg")
                        p.wait(t, "act")
                        p.wait(t, "dve")
                        for dc in range(NC):
                            mm(p, t, bank, wsl[:, dc * 128:(dc + 1) * 128],
                               xn[:, dc * 512:(dc + 1) * 512],
                               dc == 0, dc == NC - 1)
                        p.inc(t.nop(), "pe")

                    @p.block.scalar
                    def _(s, bank=bank, act=act):
                        p.wait(s, "pe")
                        p.wait(s, "qg")
                        p.inc(s.activation(ovec[:], bank, act), "act")

                    if out_d is not None:
                        @p.block.gpsimd
                        def _(g, out_d=out_d, oc=oc, tt=tt):
                            p.wait(g, "act")
                            p.inc(g.dma_start(
                                out=out_d[oc * 128:(oc + 1) * 128,
                                          tt * 512:(tt + 1) * 512],
                                in_=ovec[:]), "qg", 16)

                    if tm_d is not None:
                        @p.block.tensor
                        def _(t):
                            p.wait(t, "act")
                            for ts_ in range(4):
                                tr(p, t, psb[:, ts_ * 128:(ts_ + 1) * 128],
                                   ovec[:, ts_ * 128:(ts_ + 1) * 128],
                                   ident[:])
                            p.inc(t.nop(), "pe")

                        @p.block.scalar
                        def _(s):
                            p.wait(s, "pe")
                            p.wait(s, "qg")
                            p.inc(s.copy(otr[:], psb[:]), "act")

                        @p.block.gpsimd
                        def _(g, tm_d=tm_d, oc=oc, tt=tt):
                            p.wait(g, "act")
                            for ts_ in range(4):
                                tok0 = tt * 512 + ts_ * 128
                                p.inc(g.dma_start(
                                    out=tm_d[tok0:tok0 + 128,
                                             oc * 128:(oc + 1) * 128],
                                    in_=otr[:, ts_ * 128:(ts_ + 1) * 128]),
                                    "qg", 16)

        for (bi, s_out) in (((0, slo), (1, shi)) if upto >= 5 else ()):
            @p.block.gpsimd
            def _(g, bi=bi):
                p.wait(g, "pe")
                p.wait(g, "dve")
                p.wait(g, "qg")
                for jc in range(8):
                    tok0 = bi * 1024 + jc * 128
                    p.inc(g.dma_start(
                        out=kblk[:, jc * 2048:(jc + 1) * 2048],
                        in_=ktm[tok0:tok0 + 128, :]), "qg", 16)
                    p.inc(g.dma_start(
                        out=vblk[:, jc * 2048:(jc + 1) * 2048],
                        in_=vtm[tok0:tok0 + 128, :]), "qg", 16)

            for dc in range(NC):
                for et in range(4):
                    bank = psum[:, et * 512:et * 512 + 512]

                    @p.block.tensor
                    def _(t, dc=dc, et=et, bank=bank):
                        p.wait(t, "qg")
                        p.wait(t, "act")
                        for jc in range(8):
                            mm(p, t, bank,
                               kblk[:, jc * 2048 + dc * 128:
                                    jc * 2048 + (dc + 1) * 128],
                               vblk[:, jc * 2048 + et * 512:
                                    jc * 2048 + (et + 1) * 512],
                               jc == 0, jc == 7)
                        p.inc(t.nop(), "pe")

                    @p.block.scalar
                    def _(s, bank=bank):
                        p.wait(s, "pe")
                        p.wait(s, "qg")
                        p.inc(s.copy(ovec[:], bank), "act")

                    @p.block.gpsimd
                    def _(g, s_out=s_out, dc=dc, et=et):
                        p.wait(g, "act")
                        p.inc(g.dma_start(
                            out=s_out[dc * 128:(dc + 1) * 128,
                                      et * 512:(et + 1) * 512],
                            in_=ovec[:]), "qg", 16)

        @p.block.gpsimd
        def _(g):
            p.wait(g, "qg")

    return nc



def build_launch_B(upto=99, scansub=9):
    nc = bass.Bass("TRN2", target_bir_lowering=False, debug=False)
    x1 = nc.dram_tensor("x1", [D, TOK], BF16, kind="ExternalInput").ap()
    qT = nc.dram_tensor("qT", [D, TOK], BF16, kind="ExternalInput").ap()
    kT = nc.dram_tensor("kT", [D, TOK], BF16, kind="ExternalInput").ap()
    gT = nc.dram_tensor("gT", [D, TOK], BF16, kind="ExternalInput").ap()
    ktm = nc.dram_tensor("ktm", [TOK, D], BF16, kind="ExternalInput").ap()
    vtm = nc.dram_tensor("vtm", [TOK, D], BF16, kind="ExternalInput").ap()
    plo = nc.dram_tensor("plo", [7, D, D], BF16, kind="ExternalInput").ap()
    pmid = nc.dram_tensor("pmid", [6, D, D], BF16, kind="ExternalInput").ap()
    wpt = nc.dram_tensor("wpt", [D, D], F32, kind="ExternalInput").ap()
    wgt = nc.dram_tensor("wgt", [D, 5504], BF16, kind="ExternalInput").ap()
    wut = nc.dram_tensor("wut", [D, 5504], BF16, kind="ExternalInput").ap()
    wdt = nc.dram_tensor("wdt", [5504, D], BF16, kind="ExternalInput").ap()
    w8g = nc.dram_tensor("w8g", [D, 8192], I8, kind="ExternalInput").ap()
    w8u = nc.dram_tensor("w8u", [D, 8192], I8, kind="ExternalInput").ap()
    w8d = nc.dram_tensor("w8d", [8192, D], I8, kind="ExternalInput").ap()
    sg2 = nc.dram_tensor("sg2", [128, 64], F32, kind="ExternalInput").ap()
    su2 = nc.dram_tensor("su2", [128, 64], F32, kind="ExternalInput").ap()
    sd2 = nc.dram_tensor("sd2", [128, 16], F32, kind="ExternalInput").ap()
    wq5 = nc.dram_tensor("wq5", [D, D], F32, kind="ExternalInput").ap()
    wkv5 = nc.dram_tensor("wkv5", [D, 1024], F32, kind="ExternalInput").ap()

    x2d = nc.dram_tensor("x2d", [D, TOK], BF16, kind="Internal").ap()
    x4 = nc.dram_tensor("x4", [D, TOK], BF16, kind="ExternalOutput").ap()
    q5T = nc.dram_tensor("q5T", [D, TOK], BF16, kind="ExternalOutput").ap()
    k5T = nc.dram_tensor("k5T", [512, TOK], BF16, kind="ExternalOutput").ap()
    v5tm = nc.dram_tensor("v5tm", [TOK, 512], BF16, kind="ExternalOutput").ap()

    with contextlib.ExitStack() as stack:
        ctx = stack.enter_context
        xn = ctx(nc.sbuf_tensor("xn", [128, NC * 512], BF16))
        sq = ctx(nc.sbuf_tensor("sq", [128, NC * 512], BF16))
        wbuf = ctx(nc.sbuf_tensor("wbuf", [128, 8192], BF16))
        ab = ctx(nc.sbuf_tensor("ab", [1, 2 * TOK], F32))
        tmp_row = ctx(nc.sbuf_tensor("tmp_row", [1, 2048], F32))
        eps_row = ctx(nc.sbuf_tensor("eps_row", [1, 2], F32))
        ident = ctx(nc.sbuf_tensor("ident", [128, 128], BF16))
        ones_col = ctx(nc.sbuf_tensor("ones_col", [128, 1], BF16))
        ones_row = ctx(nc.sbuf_tensor("ones_row", [1, 128], F32))
        ovec = ctx(nc.sbuf_tensor("ovec", [128, 512], BF16))
        otr = ctx(nc.sbuf_tensor("otr", [128, 512], BF16))
        masks = ctx(nc.sbuf_tensor("masks", [128, 512], BF16))
        scl = ctx(nc.sbuf_tensor("scl", [128, 144], F32))
        psum = ctx(nc.psum_tensor("psum", [128, 3584], F32))
        psb = ctx(nc.psum_tensor("psb", [128, 512], BF16))
        block = ctx(nc.Block())
        p = P(nc, block, stack)
        for s_ in ("qg", "pe", "act", "dve", "gp"):
            p.sem(s_)

        emit_consts(p, ident, ones_col, ones_row, eps_row)

        @p.block.gpsimd
        def _(g):
            g.memset(masks[:], 1.0)
            g.affine_select(masks[:, 0:256], masks[:, 0:256], [[1, 256]],
                            ALU.is_ge, 0.0, base=0, channel_multiplier=-1)
            g.affine_select(masks[:, 256:512], masks[:, 256:512], [[1, 256]],
                            ALU.is_ge, 0.0, base=-128, channel_multiplier=-1)
            p.inc(g.dma_start(out=scl[:, 0:64], in_=sg2[:, :]), "qg", 16)
            p.inc(g.dma_start(out=scl[:, 64:128], in_=su2[:, :]), "qg", 16)
            p.inc(g.dma_start(out=scl[:, 128:144], in_=sd2[:, :]), "qg", 16)

        @p.block.vector
        def _(v):
            p.wait(v, "qg")
            p.wait(v, "gp")
            v.tensor_scalar_mul(scl[:, 128:144], scl[:, 128:144], 0.5)
            p.inc(v.engine_nop(), "dve")

        with contextlib.ExitStack() as st1:
            kv = st1.enter_context(nc.sbuf_tensor("kv", [128, NC * 2048], BF16))
            cbuf = st1.enter_context(nc.sbuf_tensor("cbuf", [128, 24576], BF16))
            qTc = cbuf[:, 0:4096]
            gTc = cbuf[:, 4096:8192]
            ktm_c = cbuf[:, 8192:12288]
            vtm_c = cbuf[:, 12288:16384]
            spm = cbuf[:, 16384:16896]
            ogc = cbuf[:, 16896:20992]
            x1c = cbuf[:, 20992:24576]

            @p.block.gpsimd
            def _(g):
                for dc in range(NC):
                    p.inc(g.dma_start(
                        out=kv[:, dc * 2048:(dc + 1) * 2048],
                        in_=plo[0, dc * 128:(dc + 1) * 128, :]), "qg", 16)

            def add_pref(src, i):
                for dc in range(NC):
                    @p.block.gpsimd
                    def _(g, dc=dc):
                        p.wait(g, "dve")
                        p.inc(g.dma_start(
                            out=sq[:, 0:2048] if dc % 2 == 0 else sq[:, 2048:4096],
                            in_=src[i, dc * 128:(dc + 1) * 128, :]), "qg", 16)

                    @p.block.vector
                    def _(v, dc=dc):
                        p.wait(v, "qg")
                        ksl = kv[:, dc * 2048:(dc + 1) * 2048]
                        ssl = sq[:, 0:2048] if dc % 2 == 0 else sq[:, 2048:4096]
                        v.tensor_add(ksl, ksl, ssl)
                        p.inc(v.engine_nop(), "dve")

            for i in range(1, 7):
                add_pref(plo, i)

            for ci in range(8 if upto >= 2 else 0):
                if ci == 4:
                    for i in range(6):
                        add_pref(pmid, i)
                t0 = ci * 256

                @p.block.gpsimd
                def _(g, t0=t0):
                    p.wait(g, "pe")
                    p.wait(g, "dve")
                    for dc in range(NC):
                        p.inc(g.dma_start(
                            out=qTc[:, dc * 256:(dc + 1) * 256],
                            in_=qT[dc * 128:(dc + 1) * 128, t0:t0 + 256]), "qg", 16)
                        p.inc(g.dma_start(
                            out=gTc[:, dc * 256:(dc + 1) * 256],
                            in_=gT[dc * 128:(dc + 1) * 128, t0:t0 + 256]), "qg", 16)
                        p.inc(g.dma_start(
                            out=xn[:, dc * 256:(dc + 1) * 256],
                            in_=kT[dc * 128:(dc + 1) * 128, t0:t0 + 256]), "qg", 16)
                    for jm in range(2):
                        p.inc(g.dma_start(
                            out=ktm_c[:, jm * 2048:(jm + 1) * 2048],
                            in_=ktm[t0 + jm * 128:t0 + (jm + 1) * 128, :]), "qg", 16)
                        p.inc(g.dma_start(
                            out=vtm_c[:, jm * 2048:(jm + 1) * 2048],
                            in_=vtm[t0 + jm * 128:t0 + (jm + 1) * 128, :]), "qg", 16)

                for jm in range(2 if scansub >= 1 else 0):
                    bank = psum[:, 2048 + jm * 256:2048 + (jm + 1) * 256]

                    @p.block.tensor
                    def _(t, jm=jm, bank=bank):
                        p.wait(t, "qg")
                        p.wait(t, "dve")
                        for dc in range(NC):
                            mm(p, t, bank,
                               xn[:, dc * 256 + jm * 128:dc * 256 + (jm + 1) * 128],
                               qTc[:, dc * 256:(dc + 1) * 256],
                               dc == 0, dc == NC - 1, nop=NOPS)
                        p.inc(t.nop(), "pe")

                if scansub >= 1:
                    @p.block.vector
                    def _(v):
                        p.wait(v, "pe")
                        for jm in range(2):
                            bank = psum[:, 2048 + jm * 256:2048 + (jm + 1) * 256]
                            v.tensor_mul(spm[:, jm * 256:(jm + 1) * 256], bank,
                                         masks[:, jm * 256:(jm + 1) * 256])
                        p.inc(v.engine_nop(), "dve")

                for eh in range(2 if scansub >= 2 else 0):
                    for ec8 in range(8):
                        ec = eh * 8 + ec8
                        bank = psum[:, ec8 * 256:(ec8 + 1) * 256]

                        @p.block.tensor
                        def _(t, ec=ec, ec8=ec8, bank=bank):
                            if ec8 == 0:
                                p.wait(t, "dve")
                            first = True
                            for jm in range(2):
                                mm(p, t, bank,
                                   vtm_c[:, jm * 2048 + ec * 128:
                                         jm * 2048 + (ec + 1) * 128],
                                   spm[:, jm * 256:(jm + 1) * 256],
                                   first, False, nop=NOPS)
                                first = False
                            for dc in range(NC):
                                mm(p, t, bank,
                                   kv[:, dc * 2048 + ec * 128:
                                      dc * 2048 + (ec + 1) * 128],
                                   qTc[:, dc * 256:(dc + 1) * 256],
                                   False, dc == NC - 1, nop=NOPS)
                            p.inc(t.nop(), "pe")

                    @p.block.vector
                    def _(v, eh=eh):
                        p.wait(v, "pe")
                        for ec8 in range(8):
                            ec = eh * 8 + ec8
                            bank = psum[:, ec8 * 256:(ec8 + 1) * 256]
                            v.tensor_mul(ogc[:, ec * 256:(ec + 1) * 256], bank,
                                         gTc[:, ec * 256:(ec + 1) * 256])
                        p.inc(v.engine_nop(), "dve")

                for mc in range(NC if scansub >= 3 else 0):
                    bank = psum[:, 2048 + (mc % 2) * 512:2048 + (mc % 2) * 512 + 256]
                    wsl = wbuf[:, (mc % 2) * 2048:(mc % 2) * 2048 + 2048]

                    @p.block.gpsimd
                    def _(g, mc=mc, wsl=wsl, t0=t0):
                        p.wait(g, "pe")
                        wr = wpt.rearrange("(c k) o -> k c o", k=128)
                        p.inc(g.dma_start(
                            out=wsl.rearrange("k (c m) -> k c m", c=16),
                            in_=wr[:, :, mc * 128:(mc + 1) * 128]), "qg", 16)
                        p.inc(g.dma_start(
                            out=x1c[:, (mc % 2) * 256:(mc % 2) * 256 + 256],
                            in_=x1[mc * 128:(mc + 1) * 128, t0:t0 + 256]), "qg", 16)

                    @p.block.tensor
                    def _(t, mc=mc, bank=bank, wsl=wsl):
                        p.wait(t, "qg")
                        p.wait(t, "dve")
                        for ec in range(NC):
                            mm(p, t, bank, wsl[:, ec * 128:(ec + 1) * 128],
                               ogc[:, ec * 256:(ec + 1) * 256],
                               ec == 0, ec == NC - 1, nop=NOPS)
                        p.inc(t.nop(), "pe")

                    @p.block.vector
                    def _(v, mc=mc, bank=bank):
                        p.wait(v, "pe")
                        p.wait(v, "qg")
                        v.tensor_add(ovec[:, (mc % 2) * 256:(mc % 2) * 256 + 256],
                                     bank,
                                     x1c[:, (mc % 2) * 256:(mc % 2) * 256 + 256])
                        p.inc(v.engine_nop(), "dve")

                    @p.block.gpsimd
                    def _(g, mc=mc, t0=t0):
                        p.wait(g, "dve")
                        p.inc(g.dma_start(
                            out=x2d[mc * 128:(mc + 1) * 128, t0:t0 + 256],
                            in_=ovec[:, (mc % 2) * 256:(mc % 2) * 256 + 256]), "qg", 16)

                if ci != 7 and scansub >= 4:
                    for dc in range(NC):
                        @p.block.tensor
                        def _(t, dc=dc):
                            p.wait(t, "dve")
                            for et in range(4):
                                bank = psum[:, et * 512:(et + 1) * 512]
                                for jm in range(2):
                                    mm(p, t, bank,
                                       ktm_c[:, jm * 2048 + dc * 128:
                                             jm * 2048 + (dc + 1) * 128],
                                       vtm_c[:, jm * 2048 + et * 512:
                                             jm * 2048 + (et + 1) * 512],
                                       jm == 0, jm == 1)
                            p.inc(t.nop(), "pe")

                        @p.block.vector
                        def _(v, dc=dc):
                            p.wait(v, "pe")
                            for et in range(4):
                                bank = psum[:, et * 512:(et + 1) * 512]
                                ksl = kv[:, dc * 2048 + et * 512:
                                         dc * 2048 + (et + 1) * 512]
                                v.tensor_add(ksl, ksl, bank)
                            p.inc(v.engine_nop(), "dve")

        with contextlib.ExitStack() as st2:
            xs = st2.enter_context(nc.sbuf_tensor("xs", [128, NC * TOK], BF16))
            hscr = st2.enter_context(nc.sbuf_tensor("hscr", [128, 32768], BF16))

            @p.block.gpsimd
            def _(g):
                p.wait(g, "qg")
                p.wait(g, "pe")
                p.wait(g, "act")
                p.wait(g, "dve")

            emit_load_xs(p, xs, x2d)

            @p.block.vector
            def _(v):
                p.wait(v, "qg")
                p.inc(v.engine_nop(), "dve")

            if upto >= 3:
                emit_ln(p, xs, sq, ab, tmp_row, ones_col, psum, eps_row)
                emit_ffn(p, xs, ab, xn, hscr[:, 0:43 * 512], wbuf, psum, ones_row,
                         "ghost", 43, wgt, wut, wdt)
            if upto >= 4:
                emit_ln(p, xs, sq, ab, tmp_row, ones_col, psum, eps_row)
                emit_ffn(p, xs, ab, xn, hscr[:, 0:64 * 512], wbuf, psum, ones_row,
                         "int8", 64, w8g, w8u, w8d,
                         sg=scl[:, 0:64], su=scl[:, 64:128], sd=scl[:, 128:144])
            emit_store_xs(p, xs, x4)

            emit_ln(p, xs, sq, ab, tmp_row, ones_col, psum, eps_row)
            projs = [(wq5, 16, q5T, None),
                     (wkv5, 8, k5T, v5tm)]
            for tt in range(TT if upto >= 5 else 0):
                emit_ln_apply(p, xs, ab, xn, ones_row, psum, tt)
                for (w_d, nocs, out_d, tm_d) in projs:
                    for oc in range(nocs):
                        par = oc % 2
                        wsl = wbuf[:, par * 2048:par * 2048 + 2048]
                        bank = psum[:, par * 512:par * 512 + 512]
                        is_v = (tm_d is not None and oc >= 4)

                        @p.block.gpsimd
                        def _(g, w_d=w_d, oc=oc, wsl=wsl):
                            p.wait(g, "pe")
                            wr = w_d.rearrange("(c k) o -> k c o", k=128)
                            p.inc(g.dma_start(
                                out=wsl.rearrange("k (c m) -> k c m", c=16),
                                in_=wr[:, :, oc * 128:(oc + 1) * 128]), "qg", 16)

                        @p.block.tensor
                        def _(t, oc=oc, wsl=wsl, bank=bank):
                            p.wait(t, "qg")
                            p.wait(t, "act")
                            p.wait(t, "dve")
                            for dc in range(NC):
                                mm(p, t, bank, wsl[:, dc * 128:(dc + 1) * 128],
                                   xn[:, dc * 512:(dc + 1) * 512],
                                   dc == 0, dc == NC - 1)
                            p.inc(t.nop(), "pe")

                        @p.block.scalar
                        def _(s, bank=bank):
                            p.wait(s, "pe")
                            p.wait(s, "qg")
                            p.inc(s.copy(ovec[:], bank), "act")

                        if not is_v:
                            @p.block.gpsimd
                            def _(g, out_d=out_d, oc=oc, tt=tt):
                                p.wait(g, "act")
                                p.inc(g.dma_start(
                                    out=out_d[oc * 128:(oc + 1) * 128,
                                              tt * 512:(tt + 1) * 512],
                                    in_=ovec[:]), "qg", 16)
                        else:
                            @p.block.tensor
                            def _(t):
                                p.wait(t, "act")
                                for ts_ in range(4):
                                    tr(p, t, psb[:, ts_ * 128:(ts_ + 1) * 128],
                                       ovec[:, ts_ * 128:(ts_ + 1) * 128],
                                       ident[:])
                                p.inc(t.nop(), "pe")

                            @p.block.scalar
                            def _(s):
                                p.wait(s, "pe")
                                p.wait(s, "qg")
                                p.inc(s.copy(otr[:], psb[:]), "act")

                            @p.block.gpsimd
                            def _(g, oc=oc, tt=tt):
                                p.wait(g, "act")
                                for ts_ in range(4):
                                    tok0 = tt * 512 + ts_ * 128
                                    p.inc(g.dma_start(
                                        out=v5tm[tok0:tok0 + 128,
                                                 (oc - 4) * 128:(oc - 3) * 128],
                                        in_=otr[:, ts_ * 128:(ts_ + 1) * 128]),
                                        "qg", 16)

        @p.block.gpsimd
        def _(g):
            p.wait(g, "qg")

    return nc



HIST_BOUND = [24, 28, 56, 60]

def build_launch_C(upto=99, attsub=9, phases=None):
    if phases is None:
        phases = set()
        if upto >= 2: phases.add("att")
        if upto >= 3: phases.add("wc")
        if upto >= 4: phases.add("ffn")
        if upto == 31: phases.update(("wcnores",))
    phases = set(phases)
    nc = bass.Bass("TRN2", target_bir_lowering=False, debug=False)
    x4 = nc.dram_tensor("x4", [D, TOK], BF16, kind="ExternalInput").ap()
    q5T = nc.dram_tensor("q5T", [D, TOK], BF16, kind="ExternalInput").ap()
    k5o = nc.dram_tensor("k5o", [512, TOK], BF16, kind="ExternalInput").ap()
    v5o = nc.dram_tensor("v5o", [TOK, 512], BF16, kind="ExternalInput").ap()
    Kb = nc.dram_tensor("Kb", [512, 8192], BF16, kind="ExternalInput").ap()
    Vb = nc.dram_tensor("Vb", [8192, 512], BF16, kind="ExternalInput").ap()
    btbl = nc.dram_tensor("btbl", [128, 4 * 64], F32, kind="ExternalInput").ap()
    wc5 = nc.dram_tensor("wc5", [D, D], F32, kind="ExternalInput").ap()
    w8g = nc.dram_tensor("w8g", [D, 8192], I8, kind="ExternalInput").ap()
    w8u = nc.dram_tensor("w8u", [D, 8192], I8, kind="ExternalInput").ap()
    w8d = nc.dram_tensor("w8d", [8192, D], I8, kind="ExternalInput").ap()
    sg2 = nc.dram_tensor("sg2", [128, 64], F32, kind="ExternalInput").ap()
    su2 = nc.dram_tensor("su2", [128, 64], F32, kind="ExternalInput").ap()
    sd2 = nc.dram_tensor("sd2", [128, 16], F32, kind="ExternalInput").ap()

    ydram = nc.dram_tensor("ydram", [D, TOK], BF16, kind="Internal").ap()
    out = nc.dram_tensor("out", [TOK, D], F32, kind="ExternalOutput").ap()

    SCALE = 1.0 / float(np.sqrt(128.0))

    with contextlib.ExitStack() as stack:
        ctx = stack.enter_context
        xn = ctx(nc.sbuf_tensor("xn", [128, NC * 512], BF16))
        sq = ctx(nc.sbuf_tensor("sq", [128, NC * 512], BF16))
        wbuf = ctx(nc.sbuf_tensor("wbuf", [128, 8192], BF16))
        ab = ctx(nc.sbuf_tensor("ab", [1, 2 * TOK], F32))
        tmp_row = ctx(nc.sbuf_tensor("tmp_row", [1, 2048], F32))
        eps_row = ctx(nc.sbuf_tensor("eps_row", [1, 2], F32))
        ident = ctx(nc.sbuf_tensor("ident", [128, 128], BF16))
        ones_col = ctx(nc.sbuf_tensor("ones_col", [128, 1], BF16))
        ones_row = ctx(nc.sbuf_tensor("ones_row", [1, 128], F32))
        ovec = ctx(nc.sbuf_tensor("ovec", [128, 512], BF16))
        ovf = ctx(nc.sbuf_tensor("ovf", [128, 512], F32))
        scl = ctx(nc.sbuf_tensor("scl", [128, 144], F32))
        psum = ctx(nc.psum_tensor("psum", [128, 3584], F32))
        psb = ctx(nc.psum_tensor("psb", [128, 512], BF16))
        block = ctx(nc.Block())
        p = P(nc, block, stack)
        for s_ in ("qg", "pe", "act", "dve", "gp"):
            p.sem(s_)

        emit_consts(p, ident, ones_col, ones_row, eps_row)

        @p.block.gpsimd
        def _(g):
            p.inc(g.dma_start(out=scl[:, 0:64], in_=sg2[:, :]), "qg", 16)
            p.inc(g.dma_start(out=scl[:, 64:128], in_=su2[:, :]), "qg", 16)
            p.inc(g.dma_start(out=scl[:, 128:144], in_=sd2[:, :]), "qg", 16)

        @p.block.vector
        def _(v):
            p.wait(v, "qg")
            p.wait(v, "gp")
            v.tensor_scalar_mul(scl[:, 128:144], scl[:, 128:144], 0.5)
            p.inc(v.engine_nop(), "dve")

        with contextlib.ExitStack() as st1:
            kres = st1.enter_context(nc.sbuf_tensor("kres", [128, 8192], BF16))
            vres = st1.enter_context(nc.sbuf_tensor("vres", [128, 8192], BF16))
            kown = st1.enter_context(nc.sbuf_tensor("kown", [128, 4 * 2048], BF16))
            vown = st1.enter_context(nc.sbuf_tensor("vown", [128, 16 * 512], BF16))
            qh_sb = st1.enter_context(nc.sbuf_tensor("qh_sb", [128, TOK], BF16))
            pb = st1.enter_context(nc.sbuf_tensor("pb", [128, 4 * 512], BF16))
            masks = st1.enter_context(nc.sbuf_tensor("masks", [128, 4 * 512], BF16))
            btb = st1.enter_context(nc.sbuf_tensor("btb", [128, 4 * 64], F32))
            recl = st1.enter_context(nc.sbuf_tensor("recl", [1, 512], F32))

            @p.block.gpsimd
            def _(g):
                g.memset(masks[:], 1.0)
                for r in range(4):
                    g.affine_select(masks[:, r * 512:(r + 1) * 512],
                                    masks[:, r * 512:(r + 1) * 512], [[1, 512]],
                                    ALU.is_ge, 0.0, base=-r * 128,
                                    channel_multiplier=-1)
                p.inc(g.dma_start(out=btb[:], in_=btbl[:, :]), "qg", 16)
                for r in range(4):
                    p.inc(g.dma_start(out=kown[:, r * 2048:(r + 1) * 2048],
                                      in_=k5o[r * 128:(r + 1) * 128, :]), "qg", 16)
                for jc in range(NC):
                    p.inc(g.dma_start(out=vown[:, jc * 512:(jc + 1) * 512],
                                      in_=v5o[jc * 128:(jc + 1) * 128, :]), "qg", 16)

            for kvh in range(4 if "att" in phases else 0):
                @p.block.gpsimd
                def _(g, kvh=kvh):
                    p.wait(g, "pe")
                    p.inc(g.dma_start(out=kres[:],
                                      in_=Kb[kvh * 128:(kvh + 1) * 128, :]),
                          "qg", 16)
                    for jc in range(64):
                        p.inc(g.dma_start(
                            out=vres[:, jc * 128:(jc + 1) * 128],
                            in_=Vb[jc * 128:(jc + 1) * 128,
                                   kvh * 128:(kvh + 1) * 128]), "qg", 16)

                for qh in range(4):
                    h = kvh * 4 + qh

                    @p.block.gpsimd
                    def _(g, h=h):
                        p.wait(g, "pe")
                        p.inc(g.dma_start(out=qh_sb[:],
                                          in_=q5T[h * 128:(h + 1) * 128, :]),
                              "qg", 16)

                    for it in range(4):
                        i0 = it * 512
                        obank = psum[:, 0:512]
                        lrow = psum[0:1, 3072:3584]
                        njc = HIST_BOUND[it]
                        total_j = njc + 4
                        SB = [512, 1024, 1536, 2560]
                        groups = [list(range(g, min(g + 4, total_j)))
                                  for g in range(0, total_j, 4)]

                        for gi, grp in enumerate(groups):
                            @p.block.tensor
                            def _(t, it=it, grp=grp):
                                p.wait(t, "qg")
                                p.wait(t, "act")
                                p.wait(t, "dve")
                                for sl, ji in enumerate(grp):
                                    diag = ji >= njc
                                    r = ji - njc
                                    sbank = psum[:, SB[sl]:SB[sl] + 512]
                                    if diag:
                                        lhs = kown[:, kvh * 2048 + i0 + r * 128:
                                                   kvh * 2048 + i0 + (r + 1) * 128]
                                    else:
                                        lhs = kres[:, ji * 128:(ji + 1) * 128]
                                    mm(p, t, sbank, lhs, qh_sb[:, i0:i0 + 512],
                                       True, True)
                                p.inc(t.nop(), "pe")

                            @p.block.scalar
                            def _(s, it=it, grp=grp):
                                if attsub < 2:
                                    return
                                p.wait(s, "pe")
                                p.wait(s, "dve")
                                last = None
                                for sl, ji in enumerate(grp):
                                    diag = ji >= njc
                                    sbank = psum[:, SB[sl]:SB[sl] + 512]
                                    pbs = pb[:, sl * 512:(sl + 1) * 512]
                                    if diag:
                                        last = s.activation(pbs, sbank, AF.Exp,
                                                            scale=SCALE)
                                    else:
                                        last = s.activation(
                                            pbs, sbank, AF.Exp, scale=SCALE,
                                            bias=btb[:, it * 64 + ji:
                                                     it * 64 + ji + 1])
                                p.inc(last, "act")

                            if grp[-1] >= njc and attsub >= 3:
                                @p.block.vector
                                def _(v, it=it, grp=grp):
                                    p.wait(v, "act")
                                    for sl, ji in enumerate(grp):
                                        if ji >= njc:
                                            r = ji - njc
                                            pbs = pb[:, sl * 512:(sl + 1) * 512]
                                            v.tensor_mul(
                                                pbs, pbs,
                                                masks[:, r * 512:(r + 1) * 512])
                                    p.inc(v.engine_nop(), "dve")

                            @p.block.tensor
                            def _(t, it=it, grp=grp, total_j=total_j):
                                if attsub < 4:
                                    return
                                if grp[-1] >= njc and attsub >= 3:
                                    p.wait(t, "dve")
                                else:
                                    p.wait(t, "act")
                                for sl, ji in enumerate(grp):
                                    diag = ji >= njc
                                    r = ji - njc
                                    pbs = pb[:, sl * 512:(sl + 1) * 512]
                                    if diag:
                                        vl = vown[:, (i0 // 128 + r) * 512 +
                                                  kvh * 128:(i0 // 128 + r) * 512 +
                                                  (kvh + 1) * 128]
                                    else:
                                        vl = vres[:, ji * 128:(ji + 1) * 128]
                                    mm(p, t, obank, vl, pbs,
                                       ji == 0, ji == total_j - 1)
                                    mm(p, t, lrow, ones_col[:], pbs,
                                       ji == 0, ji == total_j - 1, nop=NOPS)
                                p.inc(t.nop(), "pe")

                        if attsub < 5:
                            continue

                        @p.block.scalar
                        def _(s):
                            p.wait(s, "pe")
                            p.wait(s, "dve")
                            p.inc(s.copy(tmp_row[0:1, 0:512], lrow), "act")

                        @p.block.vector
                        def _(v):
                            p.wait(v, "act")
                            v.reciprocal(recl[0:1, :], tmp_row[0:1, 0:512])
                            p.inc(v.engine_nop(), "dve")

                        @p.block.tensor
                        def _(t):
                            p.wait(t, "dve")
                            mm(p, t, psum[:, 2048:2560], ones_row[:],
                               recl[0:1, :], True, True, nop=2700)
                            p.inc(t.nop(), "pe")

                        @p.block.scalar
                        def _(s):
                            p.wait(s, "pe")
                            p.wait(s, "dve")
                            p.inc(s.copy(ovf[:], obank), "act")

                        @p.block.vector
                        def _(v):
                            p.wait(v, "act")
                            v.tensor_mul(ovec[:], ovf[:], psum[:, 2048:2560])
                            p.inc(v.engine_nop(), "dve")

                        @p.block.gpsimd
                        def _(g, h=h, it=it):
                            p.wait(g, "dve")
                            p.inc(g.dma_start(
                                out=ydram[h * 128:(h + 1) * 128,
                                          it * 512:(it + 1) * 512],
                                in_=ovec[:]), "qg", 16)

        with contextlib.ExitStack() as st2:
            xs = st2.enter_context(nc.sbuf_tensor("xs", [128, NC * TOK], BF16))
            hscr = st2.enter_context(nc.sbuf_tensor("hscr", [128, 32768], BF16))
            ybig = hscr[:, 0:NC * 512]

            @p.block.gpsimd
            def _(g):
                p.wait(g, "qg")
                p.wait(g, "pe")
                p.wait(g, "act")
                p.wait(g, "dve")

            emit_load_xs(p, xs, x4)

            if "zeroxn" in phases:
                @p.block.gpsimd
                def _(g):
                    g.memset(xn[:], 0.5)
                    p.inc(g.engine_nop(), "gp")

                @p.block.tensor
                def _(t):
                    p.wait(t, "gp")
                    p.inc(t.nop(), "pe")
            wcph = ("wc" in phases or "wcnores" in phases
                    or "wcnoy" in phases or "wcnomm" in phases)
            for tt in range(TT if wcph else 0):
                if "wcnoy" not in phases:
                    @p.block.gpsimd
                    def _(g, tt=tt):
                        p.wait(g, "pe")
                        p.wait(g, "qg")
                        for dc in range(NC):
                            p.inc(g.dma_start(
                                out=ybig[:, dc * 512:(dc + 1) * 512],
                                in_=ydram[dc * 128:(dc + 1) * 128,
                                          tt * 512:(tt + 1) * 512]), "qg", 16)

                for mc in range(NC):
                    par = mc % 2
                    wsl = wbuf[:, par * 2048:par * 2048 + 2048]
                    bank = psum[:, par * 512:par * 512 + 512]

                    @p.block.gpsimd
                    def _(g, mc=mc, wsl=wsl):
                        p.wait(g, "pe")
                        wr = wc5.rearrange("(c k) o -> k c o", k=128)
                        p.inc(g.dma_start(
                            out=wsl.rearrange("k (c m) -> k c m", c=16),
                            in_=wr[:, :, mc * 128:(mc + 1) * 128]), "qg", 16)

                    @p.block.tensor
                    def _(t, mc=mc, wsl=wsl, bank=bank):
                        p.wait(t, "qg")
                        p.wait(t, "dve")
                        if "wcnomm" not in phases:
                            rsrc = xn if "wcnoy" in phases else ybig
                            for ec in range(NC):
                                mm(p, t, bank, wsl[:, ec * 128:(ec + 1) * 128],
                                   rsrc[:, ec * 512:(ec + 1) * 512],
                                   ec == 0, ec == NC - 1)
                        p.inc(t.nop(), "pe")

                    @p.block.scalar
                    def _(s, mc=mc, bank=bank):
                        if "wcnodve" in phases:
                            return
                        p.wait(s, "pe")
                        p.wait(s, "dve")
                        p.inc(s.copy(ovf[:], bank), "act")

                    @p.block.vector
                    def _(v, mc=mc, tt=tt):
                        if "wcnodve" in phases:
                            return
                        p.wait(v, "act")
                        if "wcnomm" in phases or "wcnores" in phases \
                                or "wcnoy" in phases:
                            v.tensor_copy(ovec[:], ovf[:])
                        else:
                            xsl = xs[:, mc * TOK + tt * 512:
                                     mc * TOK + tt * 512 + 512]
                            v.tensor_add(xsl, xsl, ovf[:])
                        p.inc(v.engine_nop(), "dve")

            if "ffn" in phases:
                emit_ln(p, xs, sq, ab, tmp_row, ones_col, psum, eps_row)
                emit_ffn(p, xs, ab, xn, hscr[:, 0:64 * 512], wbuf, psum, ones_row,
                         "int8", 64, w8g, w8u, w8d,
                         sg=scl[:, 0:64], su=scl[:, 64:128], sd=scl[:, 128:144])

            for dc in range(NC):
                for tg in range(4):
                    @p.block.tensor
                    def _(t, dc=dc, tg=tg):
                        p.wait(t, "dve")
                        p.wait(t, "act")
                        for k4 in range(4):
                            ts_ = tg * 4 + k4
                            tr(p, t, psb[:, k4 * 128:(k4 + 1) * 128],
                               xs[:, dc * TOK + ts_ * 128:
                                  dc * TOK + (ts_ + 1) * 128],
                               ident[:])
                        p.inc(t.nop(), "pe")

                    @p.block.scalar
                    def _(s):
                        p.wait(s, "pe")
                        p.wait(s, "qg")
                        p.inc(s.copy(ovf[:], psb[:]), "act")

                    @p.block.gpsimd
                    def _(g, dc=dc, tg=tg):
                        p.wait(g, "act")
                        for k4 in range(4):
                            ts_ = tg * 4 + k4
                            p.inc(g.dma_start(
                                out=out[ts_ * 128 + tg * 0 + 0:
                                        ts_ * 128 + 128,
                                        dc * 128:(dc + 1) * 128],
                                in_=ovf[:, k4 * 128:(k4 + 1) * 128]), "qg", 16)

        @p.block.gpsimd
        def _(g):
            p.wait(g, "qg")

    return nc



def _shard_tokens(core):
    b = core // 4
    c = core % 4
    return b, c, 7 - c


def ghost_w_inputs(inputs, core):
    gT = np.ascontiguousarray(inputs["L0_ffn_gate_idx"].T)
    uT = np.ascontiguousarray(inputs["L0_ffn_up_idx"].T)
    dT = np.ascontiguousarray(inputs["L0_ffn_down_idx"].T)
    lut128 = np.ascontiguousarray(
        np.tile(inputs["L0_ffn_lut"].astype(np.float32), (128, 1)))
    return {
        "gidx": np.ascontiguousarray(gT[core * 256:(core + 1) * 256]),
        "uidx": np.ascontiguousarray(uT[core * 256:(core + 1) * 256]),
        "didx": np.ascontiguousarray(dT[core * 688:(core + 1) * 688]),
        "lut": lut128,
    }


def _run(nc, in_maps):
    return run_bass_kernel_spmd(nc, in_maps, core_ids=list(range(8)))


def kernel(**inputs):
    inp = {k: np.asarray(v) for k, v in inputs.items()}
    x0 = inp["L0_attn_x"].astype(np.float32, copy=False)

    resW = _run(build_launch_W(), [ghost_w_inputs(inp, c) for c in range(8)])
    WgT = np.concatenate([resW.results[c]["wg"] for c in range(8)], 0)
    WuT = np.concatenate([resW.results[c]["wu"] for c in range(8)], 0)
    WdT = np.concatenate([resW.results[c]["wd"] for c in range(8)], 0)

    WqT = np.ascontiguousarray(inp["L0_attn_Wq"].T)
    WkT = np.ascontiguousarray(inp["L0_attn_Wk"].T)
    WvT = np.ascontiguousarray(inp["L0_attn_Wv"].T)
    WgaT = np.ascontiguousarray(inp["L0_attn_Wg"].T)
    in_mapsA = []
    for c in range(8):
        b, blo, bhi = _shard_tokens(c)
        xc = np.concatenate([x0[b, blo * 1024:(blo + 1) * 1024],
                             x0[b, bhi * 1024:(bhi + 1) * 1024]], 0)
        in_mapsA.append({
            "xin": np.ascontiguousarray(xc.T),
            "wgt": WgT, "wut": WuT, "wdt": WdT,
            "wqt": WqT, "wkt": WkT, "wvt": WvT, "wgat": WgaT,
        })
    resA = _run(build_launch_A(), in_mapsA)

    def blocksum(b, j):
        owner = 4 * b + (j if j < 4 else 7 - j)
        key = "slo" if (owner % 4) == j else "shi"
        return resA.results[owner][key]

    WpT = np.ascontiguousarray(inp["L0_attn_Wp"].T)
    w8g = np.ascontiguousarray(inp["L1_ffn_w_gate"])
    w8u = np.ascontiguousarray(inp["L1_ffn_w_up"])
    w8d = np.ascontiguousarray(inp["L1_ffn_w_down"])
    sg2 = np.ascontiguousarray(
        inp["L1_ffn_s_gate"].astype(np.float32).reshape(64, 128).T)
    su2 = np.ascontiguousarray(
        inp["L1_ffn_s_up"].astype(np.float32).reshape(64, 128).T)
    sd2 = np.ascontiguousarray(
        inp["L1_ffn_s_down"].astype(np.float32).reshape(16, 128).T)
    wq5 = np.ascontiguousarray(inp["L1_attn_Wq"].astype(np.float32))
    wkv5 = np.ascontiguousarray(inp["L1_attn_Wkv"].astype(np.float32))

    in_mapsB = []
    for c in range(8):
        b, blo, bhi = _shard_tokens(c)
        rA = resA.results[c]
        plo = np.zeros((7, D, D), BF)
        for i, j in enumerate(range(0, blo)):
            plo[i] = blocksum(b, j)
        pmid = np.zeros((6, D, D), BF)
        for i, j in enumerate(range(blo + 1, bhi)):
            pmid[i] = blocksum(b, j)
        in_mapsB.append({
            "x1": rA["x1"], "qT": rA["qT"], "kT": rA["kT"], "gT": rA["gT"],
            "ktm": rA["ktm"], "vtm": rA["vtm"],
            "plo": plo, "pmid": pmid, "wpt": WpT,
            "wgt": WgT, "wut": WuT, "wdt": WdT,
            "w8g": w8g, "w8u": w8u, "w8d": w8d,
            "sg2": sg2, "su2": su2, "sd2": sd2,
            "wq5": wq5, "wkv5": wkv5,
        })
    resB = _run(build_launch_B(), in_mapsB)

    wc5 = np.ascontiguousarray(inp["L1_attn_Wc"].astype(np.float32))
    Kb_b = {}
    Vb_b = {}
    for b in range(2):
        kparts, vparts = [], []
        for j in range(8):
            owner = 4 * b + (j if j < 4 else 7 - j)
            lo = (owner % 4) == j
            sl = slice(0, 1024) if lo else slice(1024, 2048)
            kparts.append(resB.results[owner]["k5T"][:, sl])
            vparts.append(resB.results[owner]["v5tm"][sl, :])
        Kb_b[b] = np.ascontiguousarray(np.concatenate(kparts, 1))
        Vb_b[b] = np.ascontiguousarray(np.concatenate(vparts, 0))

    in_mapsC = []
    for c in range(8):
        b, blo, bhi = _shard_tokens(c)
        rB = resB.results[c]
        btbl = np.zeros((4, 64), np.float32)
        for it in range(4):
            blk = blo if it < 2 else bhi
            gbase = blk * 1024 + (it % 2) * 512
            ds = gbase // 128
            for jc in range(HIST_BOUND[it]):
                btbl[it, jc] = 0.0 if jc < ds else -300.0
        btbl = np.ascontiguousarray(
            np.tile(btbl.reshape(1, 256), (128, 1)))
        in_mapsC.append({
            "x4": rB["x4"], "q5T": rB["q5T"],
            "k5o": rB["k5T"], "v5o": rB["v5tm"],
            "Kb": Kb_b[b], "Vb": Vb_b[b], "btbl": btbl,
            "wc5": wc5,
            "w8g": w8g, "w8u": w8u, "w8d": w8d,
            "sg2": sg2, "su2": su2, "sd2": sd2,
        })
    resC = _run(build_launch_C(), in_mapsC)

    outF = np.zeros((2, 8192, D), np.float32)
    for c in range(8):
        b, blo, bhi = _shard_tokens(c)
        oc = resC.results[c]["out"]
        outF[b, blo * 1024:(blo + 1) * 1024] = oc[0:1024]
        outF[b, bhi * 1024:(bhi + 1) * 1024] = oc[1024:2048]
    return outF
